# version 31
# baseline (speedup 1.0000x reference)
import numpy as np
import ml_dtypes
import concourse.bass as bass
import concourse.mybir as mybir
from concourse.bass_utils import run_bass_kernel_spmd

F32 = mybir.dt.float32
BF16 = mybir.dt.bfloat16
I32 = mybir.dt.int32
AF = mybir.ActivationFunctionType
ALU = mybir.AluOpType

NCORE = 8
B, S, D = 2, 4096, 4096
NH, NKV, HD = 32, 8, 128
SCALE = float(HD) ** -0.5
EPS = 1e-5
NEG = -80.0
TPC = 1024
NTC = 8

ENGINES = ("pe", "act", "dve", "pool", "sp")
DEBUG_TAPS = False
E_SERIALIZE = False


class _Op:
    __slots__ = ("eng", "emit", "reads", "writes", "is_dma", "sem_val",
                 "dma_val", "dma_key", "tag", "deps", "waits", "group")

    def __init__(self, eng, emit, reads, writes, is_dma, tag, waits, group):
        self.eng, self.emit = eng, emit
        self.reads, self.writes = list(reads), list(writes)
        self.is_dma, self.tag, self.waits = is_dma, tag, list(waits)
        self.sem_val = None
        self.dma_val = None
        self.dma_key = None
        self.group = group


class Sched:

    def __init__(self, nc):
        self.nc = nc
        self.ops = {e: [] for e in ENGINES}
        self.res = {}

    def add(self, eng, emit, reads=(), writes=(), is_dma=False, tag="",
            waits=(), group=None):
        op = _Op(eng, emit, reads, writes, is_dma, tag, waits, group)
        excl = [r for r in op.reads if r.startswith("ps")]
        if excl:
            op.reads = [r for r in op.reads if not r.startswith("ps")]
            op.writes = op.writes + [r for r in excl if r not in op.writes]
        self.ops[eng].append(op)
        deps = []
        for r in op.reads:
            w, _ = self.res.get(r, (None, []))
            if w is not None:
                deps.append(w)
        for r in op.writes:
            w, readers = self.res.get(r, (None, []))
            if w is not None and not (
                    op.is_dma and w.is_dma and w.eng == op.eng
                    and op.group is not None and w.group == op.group):
                deps.append(w)
            deps.extend(readers)
        seen = set()
        op.deps = []
        for d in deps:
            if d is op or id(d) in seen:
                continue
            seen.add(id(d))
            op.deps.append(d)
        for r in op.reads:
            w, readers = self.res.get(r, (None, []))
            readers = [x for x in readers if x.eng != op.eng] + [op]
            self.res[r] = (w, readers)
        for r in op.writes:
            self.res[r] = (op, [])
        return op

    def dma(self, eng, emit, reads=(), writes=(), tag="", waits=(),
            group=None):
        return self.add(eng, emit, reads, writes, is_dma=True, tag=tag,
                        waits=waits, group=group)

    def emit(self):
        nc = self.nc
        needed = set()
        for e in ENGINES:
            for op in self.ops[e]:
                for d in op.deps:
                    if d.eng == op.eng == "pe" and not d.is_dma and not op.is_dma:
                        continue
                    if d.is_dma:
                        continue
                    needed.add(id(d))
        eng_counts = {e: 0 for e in ENGINES}
        dma_counts = {}
        for e in ENGINES:
            for op in self.ops[e]:
                if op.is_dma:
                    key = op.group or (op.writes[0] if op.writes else "_misc")
                    op.dma_key = key
                    dma_counts[key] = dma_counts.get(key, 0) + 16
                    op.dma_val = dma_counts[key]
                if id(op) in needed:
                    eng_counts[e] += 1
                    op.sem_val = eng_counts[e]
        sems, dsems, ctxs = {}, {}, []
        for e in ENGINES:
            cm = nc.semaphore(f"sem_{e}")
            sems[e] = cm.__enter__()
            ctxs.append(cm)
        for key in dma_counts:
            cm = nc.semaphore(f"dsem_{key}")
            dsems[key] = cm.__enter__()
            ctxs.append(cm)
        block_cm = nc.Block()
        block = block_cm.__enter__()
        ctxs.append(block_cm)

        def emit_engine(engobj, e):
            waited = {}
            for op in self.ops[e]:
                for d in op.deps:
                    if d.eng == e == "pe" and not d.is_dma and not op.is_dma:
                        continue
                    if d.is_dma:
                        key = ("d", d.dma_key)
                        if waited.get(key, 0) < d.dma_val:
                            engobj.wait_ge(dsems[d.dma_key], d.dma_val)
                            waited[key] = d.dma_val
                    else:
                        if waited.get(d.eng, 0) < d.sem_val:
                            engobj.wait_ge(sems[d.eng], d.sem_val)
                            waited[d.eng] = d.sem_val
                for (sem, val) in op.waits:
                    key = ("x", sem.name)
                    if waited.get(key, 0) < val:
                        engobj.wait_ge(sem, val)
                        waited[key] = val
                ins = op.emit(engobj)
                if op.is_dma:
                    ins.then_inc(dsems[op.dma_key], 16)
                elif op.sem_val is not None:
                    ins.then_inc(sems[e], 1)

        section = {"pe": block.tensor, "act": block.scalar,
                   "dve": block.vector, "pool": block.gpsimd,
                   "sp": block.sync}
        for e in ENGINES:
            if not self.ops[e]:
                continue
            def make(e):
                def body(engobj):
                    emit_engine(engobj, e)
                return body
            section[e](make(e))
        for c in reversed(ctxs):
            c.__exit__(None, None, None)


def _chunk_owner(c):
    return (c, 0) if c <= 3 else (7 - c, 1)


def _kg_col(c128):
    c, sub = c128 // 4, c128 % 4
    own, half = _chunk_owner(c)
    return own * 1024 + half * 512 + sub * 128


def _v_tile(c128):
    c, sub = c128 // 4, c128 % 4
    own, half = _chunk_owner(c)
    return own * 8 + half * 4 + sub


def _slot_tables():
    lo = []
    for p in range(6):
        lo.append(dict(kind="g", c128=2 * p, bcol=p))
    for pr in range(2):
        lo.append(dict(kind="d", sub=2 * pr, bcol=6 + pr))
    hi = []
    for p in range(14):
        hi.append(dict(kind="g", c128=2 * p, bcol=8 + p))
    for pr in range(2):
        hi.append(dict(kind="d", sub=2 * pr, bcol=22 + pr))
    return lo, hi


LO_SLOTS, HI_SLOTS = _slot_tables()


def build_program():
    nc = bass.Bass("TRN2", target_bir_lowering=False)
    P = 128

    x_my = nc.dram_tensor("x_my", [TPC, D], F32, kind="ExternalInput").ap()
    wq = nc.dram_tensor("wq", [D, D], F32, kind="ExternalInput").ap()
    wk = nc.dram_tensor("wk", [D, 1024], F32, kind="ExternalInput").ap()
    wv = nc.dram_tensor("wv", [D, 1024], F32, kind="ExternalInput").ap()
    wo = nc.dram_tensor("wo", [D, D], F32, kind="ExternalInput").ap()
    wqnt = nc.dram_tensor("wqnt", [D, D], I32, kind="ExternalInput").ap()
    qsc = nc.dram_tensor("qsc", [D, 64], F32, kind="ExternalInput").ap()
    qzp = nc.dram_tensor("qzp", [D, 64], F32, kind="ExternalInput").ap()
    fbias = nc.dram_tensor("fbias", [1, D], F32, kind="ExternalInput").ap()
    btab_in = nc.dram_tensor("btab_in", [P, 24], F32, kind="ExternalInput").ap()
    masks_in = nc.dram_tensor("masks_in", [P, 2, 1024], BF16,
                              kind="ExternalInput").ap()
    out_my = nc.dram_tensor("out_my", [TPC, D], F32, kind="ExternalOutput").ap()
    if DEBUG_TAPS:
        dbg_xn = nc.dram_tensor("dbg_xn", [TPC, D], BF16, kind="ExternalOutput").ap()
        dbg_q = nc.dram_tensor("dbg_q", [TPC, D], BF16, kind="ExternalOutput").ap()
        dbg_kv = nc.dram_tensor("dbg_kv", [2048, 1024], BF16, kind="ExternalOutput").ap()
        dbg_kvo = nc.dram_tensor("dbg_kvo", [8192, 1024], BF16, kind="ExternalOutput").ap()
        dbg_wdq = nc.dram_tensor("dbg_wdq", [D, D], BF16, kind="ExternalOutput").ap()
        dbg_ao = nc.dram_tensor("dbg_ao", [128, 32768], BF16, kind="ExternalOutput").ap()
        dbg_s = nc.dram_tensor("dbg_s", [TPC, D], F32, kind="ExternalOutput").ap()
        dbg_g = nc.dram_tensor("dbg_g", [TPC, D], F32, kind="ExternalOutput").ap()
        dbg_wob = nc.dram_tensor("dbg_wob", [128, 8192], BF16, kind="ExternalOutput").ap()
        dbg_wt = nc.dram_tensor("dbg_wt", [128, 8192], BF16, kind="ExternalOutput").ap()

    xn_scr = nc.dram_tensor("xn_scr", [TPC, D], BF16).ap()
    q_scr = nc.dram_tensor("q_scr", [TPC, D], BF16).ap()
    kv_cin = nc.dram_tensor("kv_cin", [2048, 1024], BF16)
    kv_cout = nc.dram_tensor("kv_cout", [8192, 1024], BF16)
    wdq_scr = nc.dram_tensor("wdq_scr", [D, D], BF16).ap()
    wdqT_scr = nc.dram_tensor("wdqT_scr", [D, D], BF16).ap()

    from contextlib import ExitStack
    with ExitStack() as ctx:
        def sb(name, shape, dt):
            return ctx.enter_context(nc.sbuf_tensor(name, shape, dt))

        def psum(name, shape):
            return ctx.enter_context(nc.psum_tensor(name, shape, F32))

        xnT = sb("xnT", [P, 32, 1024], BF16)
        aou = sb("aou", [P, 32768], BF16)
        masks = None
        btab = sb("btab", [P, 24], F32)
        onescol = sb("onescol", [P, 1], F32)
        onescol_b = sb("onescol_b", [P, 1], BF16)
        onesrow_f = sb("onesrow_f", [1, P], F32)
        onesrow_b = sb("onesrow_b", [1, P], BF16)
        epscol = sb("epscol", [P, 1], F32)
        c_sum = sb("c_sum", [P, 2], F32)
        c_sq = sb("c_sq", [P, 2], F32)
        c_m = sb("c_m", [P, 2], F32)
        c_var = sb("c_var", [P, 2], F32)
        c_m2 = sb("c_m2", [P, 2], F32)
        c_nr = sb("c_nr", [P, 2], F32)
        c_rstd = sb("c_rstd", [P, 2], F32)
        bias_sb = sb("bias_sb", [1, 2, 256], BF16)

        abf = sb("abf", [P, 28672], BF16)
        af32 = sb("af32", [P, 2048], F32)
        d_qf = sb("d_qf", [P, 2, 1024], F32)
        d_out = sb("d_out", [P, 2, 1024], BF16)
        d_zp = sb("d_zp", [P, 1, 64], F32)
        wtt = sb("wtt", [P, 2, 256], BF16)
        d_sc = sb("d_sc", [P, 1, 64], F32)

        ps_r = [psum("ps_r0", [P, 1024]), psum("ps_r1", [P, 1024]),
                psum("ps_r2", [P, 1024])]
        ps_av = psum("ps_av", [P, 1024])

        csem_cm = nc.semaphore("csem")
        csem = csem_cm.__enter__()

        sc = Sched(nc)

        aou_f32 = aou.bitcast(F32)
        xA = [aou_f32[:, 0:4096], aou_f32[:, 4096:8192]]
        xsq = aou_f32[:, 8192:12288]
        xnb = [aou[:, 24576:28672], aou[:, 28672:32768]]
        WbP = [aou[:, 0:16384].rearrange("p (dc n) -> p dc n", dc=32),
               aou[:, 16384:32768].rearrange("p (dc n) -> p dc n", dc=32)]

        def slot_ap(h):
            return aou[:, h * 1024:(h + 1) * 1024]

        Kg = [abf[:, 0:5120], abf[:, 5120:10240]]
        Vg = abf[:, 10240:15360].rearrange("p (t d) -> p t d", d=128)
        Qb = [abf[:, 15360:16384], abf[:, 22528:23552]]
        pT = [abf[:, 16384 + 1024 * j:17408 + 1024 * j] for j in range(4)]
        pT.append(abf[:, 23552:24576])
        masks = abf[:, 24576:26624].rearrange("p (a c) -> p a c", a=2)
        accZ2 = [abf[:, 20480:21504], abf[:, 21504:22528]]
        wt_ring = [abf[:, 4096 * j:4096 * (j + 1)]
                   .rearrange("p (dc n) -> p dc n", dc=16) for j in range(4)]
        wob_ring = [abf[:, 16384 + 4096 * j:16384 + 4096 * (j + 1)]
                    .rearrange("p (dc n) -> p dc n", dc=16) for j in range(3)]

        bzt = af32[:, 0:1024]
        rz = af32[0:1, 1024:2048]
        xpc = [af32[:, 0:256], af32[:, 256:512]]
        spc = af32[:, 512:768]
        gpc = af32[:, 768:1024]
        opc = [af32[:, 1024:1280], af32[:, 1280:1536]]

        sc.add("dve", lambda e: e.memset(onescol[:, :], 1.0), writes=["onescol"])
        sc.add("dve", lambda e: e.memset(onescol_b[:, :], 1.0), writes=["onescol_b"])
        sc.add("dve", lambda e: e.memset(onesrow_f[:, :], 1.0), writes=["onesrow_f"])
        sc.add("dve", lambda e: e.memset(onesrow_b[:, :], 1.0), writes=["onesrow_b"])
        sc.add("dve", lambda e: e.memset(epscol[:, :], EPS), writes=["epscol"])
        sc.dma("act", lambda e: e.dma_start(btab[:, :], btab_in), writes=["btab"])
        sc.dma("act", lambda e: e.dma_start(masks[:, :, :], masks_in),
               writes=["masks"])

        for tc in range(NTC):
            p = tc % 2
            xa, xr = xA[p], f"xA{p}"
            cs = lambda t, p=p: t[:, p:p + 1]
            sfx = str(p)
            sc.dma("sp" if p == 0 else "pool", lambda e, tc=tc, xa=xa:
                   e.dma_start(xa, x_my[tc * 128:(tc + 1) * 128, :]),
                   writes=[xr], tag=f"ldx{tc}")
            xb, xbr = xnb[p], f"xnb{p}"
            sc.add("act", lambda e, xa=xa, xb=xb, p=p: e.activation(
                       xb, xa, AF.Copy, accum_out=c_sum[:, p:p + 1]),
                   reads=[xr], writes=[xbr, "c_sum" + sfx])
            sc.add("act", lambda e, xa=xa, p=p: e.activation(
                       xsq, xa, AF.Square, accum_out=c_sq[:, p:p + 1]),
                   reads=[xr], writes=["xsq", "c_sq" + sfx])
            sc.add("dve", lambda e, p=p: e.tensor_scalar(
                       c_m[:, p:p + 1], c_sum[:, p:p + 1], 1.0 / D, None,
                       ALU.mult),
                   reads=["c_sum" + sfx], writes=["c_m" + sfx])
            sc.add("dve", lambda e, p=p: e.tensor_scalar(
                       c_var[:, p:p + 1], c_sq[:, p:p + 1], 1.0 / D, None,
                       ALU.mult),
                   reads=["c_sq" + sfx], writes=["c_var" + sfx])
            sc.add("dve", lambda e, p=p: e.tensor_tensor(
                       c_m2[:, p:p + 1], c_m[:, p:p + 1], c_m[:, p:p + 1],
                       ALU.mult),
                   reads=["c_m" + sfx], writes=["c_m2" + sfx])
            sc.add("dve", lambda e, p=p: e.tensor_tensor(
                       c_var[:, p:p + 1], c_var[:, p:p + 1], c_m2[:, p:p + 1],
                       ALU.subtract),
                   reads=["c_var" + sfx, "c_m2" + sfx], writes=["c_var" + sfx],
                   tag="var")
            sc.add("dve", lambda e, p=p: e.tensor_scalar(
                       c_var[:, p:p + 1], c_var[:, p:p + 1], EPS, None,
                       ALU.add),
                   reads=["c_var" + sfx], writes=["c_var" + sfx])
            sc.add("act", lambda e, p=p: e.activation(
                       c_rstd[:, p:p + 1], c_var[:, p:p + 1], AF.Sqrt),
                   reads=["c_var" + sfx], writes=["c_rstd" + sfx])
            sc.add("dve", lambda e, p=p: e.reciprocal(
                       c_rstd[:, p:p + 1], c_rstd[:, p:p + 1]),
                   reads=["c_rstd" + sfx], writes=["c_rstd" + sfx])
            sc.add("dve", lambda e, p=p: e.tensor_tensor(
                       c_nr[:, p:p + 1], c_rstd[:, p:p + 1],
                       c_rstd[:, p:p + 1], ALU.mult),
                   reads=["c_rstd" + sfx], writes=["c_nr" + sfx])
            sc.add("dve", lambda e, p=p: e.tensor_tensor(
                       c_nr[:, p:p + 1], c_nr[:, p:p + 1], c_var[:, p:p + 1],
                       ALU.mult),
                   reads=["c_nr" + sfx, "c_var" + sfx], writes=["c_nr" + sfx])
            sc.add("dve", lambda e, p=p: e.tensor_scalar(
                       c_nr[:, p:p + 1], c_nr[:, p:p + 1], -0.5, 1.5,
                       ALU.mult, ALU.add),
                   reads=["c_nr" + sfx], writes=["c_nr" + sfx])
            sc.add("dve", lambda e, p=p: e.tensor_tensor(
                       c_rstd[:, p:p + 1], c_rstd[:, p:p + 1], c_nr[:, p:p + 1],
                       ALU.mult),
                   reads=["c_rstd" + sfx, "c_nr" + sfx],
                   writes=["c_rstd" + sfx])
            sc.add("dve", lambda e, xa=xa, xb=xb, p=p: e.tensor_scalar(
                       xb, xa, c_m[:, p:p + 1], c_rstd[:, p:p + 1],
                       ALU.subtract, ALU.mult),
                   reads=[xr, "c_m" + sfx, "c_rstd" + sfx], writes=[xbr],
                   tag=f"xn{tc}")
            sc.dma("act", lambda e, tc=tc, xb=xb:
                   e.dma_start(xn_scr[tc * 128:(tc + 1) * 128, :], xb),
                   reads=[xbr], writes=[f"xn_s{tc % 2}"],
                   group=f"xn_s{tc % 2}", tag=f"stxn{tc}")
        for dc in range(32):
            sc.dma("act", lambda e, dc=dc: e.dma_start(
                       xnT[:, dc, :], xn_scr[:, dc * 128:(dc + 1) * 128],
                       transpose=True),
                   reads=["xn_s0", "xn_s1"], writes=[f"xnT{dc}"],
                   tag=f"trxn{dc}")

        def emit_deq(nb):
            sc.dma("pool", lambda e, nb=nb: e.dma_start(
                       d_zp[:, 0, :], qzp[nb * 128:(nb + 1) * 128, :]),
                   writes=["dzp"], tag=f"dzp{nb}")
            sc.dma("pool", lambda e, nb=nb: e.dma_start(
                       d_sc[:, 0, :], qsc[nb * 128:(nb + 1) * 128, :]),
                   writes=["dsc"], tag=f"dsc{nb}")
            for qq in range(4):
                p = qq % 2
                sc.dma("pool", lambda e, nb=nb, qq=qq, p=p: e.dma_start(
                           d_qf[:, p, :],
                           wqnt[nb * 128:(nb + 1) * 128,
                                qq * 1024:(qq + 1) * 1024]),
                       writes=[f"dI{p}"], tag=f"dq{nb}_{qq}")
                g0 = qq * 16
                zpb = d_zp[:, 0, :].rearrange("p (g u) -> p g u", u=1)[
                    :, g0:g0 + 16, :].broadcast_to([P, 16, 64])
                scb = d_sc[:, 0, :].rearrange("p (g u) -> p g u", u=1)[
                    :, g0:g0 + 16, :].broadcast_to([P, 16, 64])
                qfv = d_qf[:, p, :].rearrange("p (g u) -> p g u", g=16)
                outv = d_out[:, p, :].rearrange("p (g u) -> p g u", g=16)
                sc.add("pool", lambda e, qfv=qfv, zpb=zpb:
                       e.tensor_tensor(qfv, qfv, zpb, ALU.subtract),
                       reads=[f"dI{p}", "dzp"], writes=[f"dI{p}"],
                       tag=f"dqs{nb}_{qq}")
                sc.add("pool", lambda e, outv=outv, qfv=qfv, scb=scb:
                       e.tensor_tensor(outv, qfv, scb, ALU.mult),
                       reads=[f"dI{p}", "dsc"], writes=[f"dO{p}"],
                       tag=f"dqm{nb}_{qq}")
                sc.dma("pool", lambda e, nb=nb, qq=qq, p=p: e.dma_start(
                           wdq_scr[nb * 128:(nb + 1) * 128,
                                   qq * 1024:(qq + 1) * 1024],
                           d_out[:, p, :]),
                       reads=[f"dO{p}"], writes=[f"wdq_s{p}"],
                       group=f"wdq_s{p}", tag=f"dqst{nb}_{qq}")

        def emit_wdqT(n2):
            for dc in range(32):
                p = dc % 2
                sc.dma("sp", lambda e, n2=n2, dc=dc, p=p: e.dma_start(
                           wtt[:, p, :],
                           wdq_scr[n2 * 256:(n2 + 1) * 256,
                                   dc * 128:(dc + 1) * 128],
                           transpose=True),
                       reads=["wdq_s0", "wdq_s1"], writes=[f"wtt{p}"],
                       tag=f"wTt{n2}_{dc}")
                sc.dma("sp", lambda e, n2=n2, dc=dc, p=p: e.dma_start(
                           wdqT_scr[dc * 128:(dc + 1) * 128,
                                    n2 * 256:(n2 + 1) * 256],
                           wtt[:, p, :]),
                       reads=[f"wtt{p}"], writes=[f"wdqT_s{p}"],
                       group=f"wdqT_s{p}", tag=f"wTs{n2}_{dc}")


        pj_blocks = ([("k", wk, bi) for bi in range(2)]
                     + [("v", wv, bi) for bi in range(2)]
                     + [("q", wq, bi) for bi in range(8)])
        ev = [abf[:, 16384 + 512 * j:16896 + 512 * j] for j in range(6)]
        wb_fence = [["xA0", "xA1"], ["xsq", "xnb0", "xnb1"]]
        for bx, (w_kind, w_ap, bi) in enumerate(pj_blocks):
            par = bx % 2
            Wb = WbP[par]
            wn = "AB"[par]
            for dc in range(32):
                extra = wb_fence[par]
                wb_fence[par] = []
                sc.dma("pool", lambda e, dc=dc, bi=bi, w_ap=w_ap, Wb=Wb:
                       e.dma_start(Wb[:, dc, :],
                                   w_ap[dc * 128:(dc + 1) * 128,
                                        bi * 512:(bi + 1) * 512]),
                       reads=[], writes=[f"Wb{wn}"] + extra,
                       group=f"Wb{wn}", tag=f"ldw{bx}_{dc}")
            for tc in range(NTC):
                rj, rh = (tc % 6) // 2, (tc % 6) % 2
                pp = ps_r[rj][:, rh * 512:(rh + 1) * 512]
                ppr = f"ps_r{rj}{'LH'[rh]}"
                for dc in range(32):
                    sc.add("pe", lambda e, dc=dc, tc=tc, pp=pp, Wb=Wb:
                           e.matmul(pp, xnT[:, dc, tc * 128:(tc + 1) * 128],
                                    Wb[:, dc, :], start=(dc == 0),
                                    stop=(dc == 31)),
                           reads=[f"xnT{dc}", f"Wb{wn}"], writes=[ppr],
                           tag=f"pj{bx}_{tc}_{dc}")
                evb, evr = ev[tc % 6], f"ev{tc % 6}"
                sc.add("act", lambda e, evb=evb, pp=pp:
                       e.copy(evb, pp),
                       reads=[ppr], writes=[evr], tag=f"evac{bx}_{tc}")
                if w_kind == "k":
                    dst = kv_cin.ap()[tc * 128:(tc + 1) * 128,
                                      bi * 512:(bi + 1) * 512]
                    dres = f"kv_s{tc % 6}"
                elif w_kind == "v":
                    dst = kv_cin.ap()[1024 + tc * 128:1024 + (tc + 1) * 128,
                                      bi * 512:(bi + 1) * 512]
                    dres = f"kv_s{tc % 6}"
                else:
                    dst = q_scr[tc * 128:(tc + 1) * 128,
                                bi * 512:(bi + 1) * 512]
                    dres = f"q_s{tc % 6}"
                sc.dma("act", lambda e, dst=dst, evb=evb: e.dma_start(dst, evb),
                       reads=[evr], writes=[dres], group=dres,
                       tag=f"st{bx}_{tc}")
            if w_kind == "v" and bi == 1:
                def cc(e):
                    return e.collective_compute(
                        "AllGather", ALU.bypass,
                        replica_groups=[[0, 1, 2, 3], [4, 5, 6, 7]],
                        ins=[kv_cin.ap()], outs=[kv_cout.ap()],
                    ).then_inc(csem, 1)
                sc.add("pool", cc, reads=['kv_s0', 'kv_s1', 'kv_s2', 'kv_s3', 'kv_s4', 'kv_s5'], writes=[], tag="cc")

        for nb in range(32):
            emit_deq(nb)

        def emit_kv_load(kvh):
            p = kvh % 2
            kg, kgr = Kg[p], f"Kg{p}"
            w = [(csem, 1)]
            for r in range(4):
                sc.dma("sp", lambda e, r=r, kvh=kvh, kg=kg: e.dma_start(
                           kg[0:P, r * 1024:(r + 1) * 1024],
                           kv_cout.ap()[r * 2048:r * 2048 + 1024,
                                        kvh * 128:(kvh + 1) * 128],
                           transpose=True),
                       reads=[], writes=[kgr], group=kgr, tag=f"kg{kvh}_{r}",
                       waits=w)
            sc.dma("sp", lambda e, kvh=kvh, kg=kg: e.dma_start(
                       kg[0:P, 4096:5120],
                       kv_cin.ap()[0:1024, kvh * 128:(kvh + 1) * 128],
                       transpose=True),
                   reads=['kv_s0', 'kv_s1', 'kv_s2', 'kv_s3', 'kv_s4', 'kv_s5'], writes=[kgr], group=kgr,
                   tag=f"kgm{kvh}")
            for r in range(4):
                sc.dma("sp", lambda e, r=r, kvh=kvh: e.dma_start(
                           Vg[:, r * 8:(r + 1) * 8, :],
                           kv_cout.ap()[r * 2048 + 1024:(r + 1) * 2048,
                                        kvh * 128:(kvh + 1) * 128]
                           .rearrange("(a p) d -> p a d", p=128)),
                       reads=[], writes=[f"Vg{r}"], group=f"Vg{r}",
                       tag=f"vg{kvh}_{r}", waits=w)
            sc.dma("sp", lambda e, kvh=kvh: e.dma_start(
                       Vg[:, 32:40, :],
                       kv_cin.ap()[1024:2048, kvh * 128:(kvh + 1) * 128]
                       .rearrange("(a p) d -> p a d", p=128)),
                   reads=['kv_s0', 'kv_s1', 'kv_s2', 'kv_s3', 'kv_s4', 'kv_s5'],
                   writes=["Vgm"], group="Vgm", tag=f"vgm{kvh}")

        def emit_q_load(h):
            extra = ["ev0", "ev1"] if h == 0 else []
            p = h % 2
            sc.dma("sp", lambda e, h=h, p=p: e.dma_start(
                       Qb[p], q_scr[0:1024, h * 128:(h + 1) * 128],
                       transpose=True),
                   reads=['q_s0', 'q_s1', 'q_s2', 'q_s3', 'q_s4', 'q_s5'],
                   writes=[f"Qb{p}"] + extra, tag=f"qld{h}")

        emit_kv_load(0)
        emit_q_load(0)
        slotctr = [0]
        tailq = []
        ev_fence = [[f"ev{j}" for j in range(6)]]

        def emit_pass(h, kvh, qhalf, pairs, avh, inject=()):
            kp = kvh % 2
            kgr, qr = f"Kg{kp}", f"Qb{h % 2}"
            kg = Kg[kp]
            q = Qb[h % 2]
            n = len(pairs)
            LAG = 4
            avap = ps_av[:, avh * 512:(avh + 1) * 512]
            avr = f"ps_av{'LH'[avh]}"

            def pair_aps(pp_):
                out = []
                for half in range(2):
                    if pp_["kind"] == "g":
                        c128 = pp_["c128"] + half
                        kap = kg[0:P, _kg_col(c128):_kg_col(c128) + 128]
                        vap = Vg[:, _v_tile(c128), :]
                        vres = f"Vg{_v_tile(c128) // 8}"
                    else:
                        sub = pp_["sub"] + half
                        koff = 4096 + qhalf * 512 + sub * 128
                        kap = kg[0:P, koff:koff + 128]
                        vap = Vg[:, 32 + qhalf * 4 + sub, :]
                        vres = "Vgm"
                    out.append((kap, vap, vres))
                return out

            def emit_sc(px):
                pp_ = pairs[px]
                j = slotctr[0] % 3
                jt = slotctr[0] % 5
                rr = ps_r[j]
                pt = pT[jt]
                pair_ps[px] = (j, jt, rr, pt)
                slotctr[0] += 1
                aps = pair_aps(pp_)
                for half in range(2):
                    kap = aps[half][0]
                    sc.add("pe", lambda e, kap=kap, rr=rr, half=half:
                           e.matmul(rr[:, half * 512:(half + 1) * 512], kap,
                                    q[0:P, qhalf * 512:(qhalf + 1) * 512],
                                    start=True, stop=True),
                           reads=[kgr, qr], writes=[f"ps_r{j}{'LH'[half]}"],
                           tag=f"sc{h}_{qhalf}_{px}_{half}")
                pt_extra = ev_fence[0]
                ev_fence[0] = []
                sc.add("act", lambda e, pt=pt, rr=rr, pp_=pp_:
                       e.activation(pt, rr[:, :], AF.Exp,
                                    bias=btab[:, pp_["bcol"]:pp_["bcol"] + 1],
                                    scale=SCALE),
                       reads=[f"ps_r{j}L", f"ps_r{j}H", "btab"],
                       writes=[f"pT{jt}"] + pt_extra,
                       tag=f"exp{h}_{qhalf}_{px}")
                if pp_["kind"] == "d":
                    pr = pp_["sub"] // 2
                    sc.add("dve", lambda e, pt=pt, pr=pr:
                           e.tensor_tensor(pt, pt, masks[:, pr, :], ALU.mult),
                           reads=[f"pT{jt}", "masks"], writes=[f"pT{jt}"],
                           tag=f"msk{h}_{qhalf}_{px}")
                accv = accZ2[h % 2]
                accr = f"accZ{h % 2}"
                for half in range(2):
                    sc.add("dve", lambda e, pt=pt, half=half, accv=accv:
                           e.tensor_tensor(
                               accv[0:P, qhalf * 512:(qhalf + 1) * 512],
                               accv[0:P, qhalf * 512:(qhalf + 1) * 512],
                               pt[0:P, half * 512:(half + 1) * 512], ALU.add),
                           reads=[f"pT{jt}", accr], writes=[accr],
                           tag=f"zac{h}_{qhalf}_{px}_{half}")

            def emit_av(px):
                pp_ = pairs[px]
                j, jt, rr, pt = pair_ps[px]
                aps = pair_aps(pp_)
                for half in range(2):
                    _, vap, vres = aps[half]
                    sx = 2 * px + half
                    sc.add("pe", lambda e, vap=vap, pt=pt, half=half,
                           sx=sx, avap=avap:
                           e.matmul(avap, vap,
                                    pt[0:P, half * 512:(half + 1) * 512],
                                    start=(sx == 0), stop=(sx == 2 * n - 1)),
                           reads=[vres, f"pT{jt}"], writes=[avr],
                           tag=f"av{h}_{qhalf}_{sx}")

            pair_ps = {}
            inj = dict(inject)
            for px in range(n + LAG):
                if px < n:
                    emit_sc(px)
                if px in inj:
                    inj[px]()
                if px >= LAG:
                    emit_av(px - LAG)

        for kvh in range(NKV):
            for qq in range(4):
                h = kvh * 4 + qq
                sc.add("dve", lambda e, h=h: e.memset(accZ2[h % 2][0:P, :], 0.0),
                       reads=[], writes=[f"accZ{h % 2}"], tag=f"zclr{h}")
                inj = []
                if tailq:
                    ta, tb = tailq.pop(0)
                    inj = [(0, ta), (2, tb)]
                emit_pass(h, kvh, 0, LO_SLOTS, 0, inject=inj)
                emit_pass(h, kvh, 1, HI_SLOTS, 1)

                def make_tail(h, j):
                    rr = ps_r[j]
                    accv = accZ2[h % 2]
                    accr = f"accZ{h % 2}"

                    def tail_a():
                        for qhalf in range(2):
                            rres = f"ps_r{j}{'LH'[qhalf]}"
                            sc.add("pe", lambda e, qhalf=qhalf, rr=rr,
                                   accv=accv: e.matmul(
                                       rr[0:1, qhalf * 512:(qhalf + 1) * 512],
                                       onescol_b[:, :],
                                       accv[0:P, qhalf * 512:(qhalf + 1) * 512],
                                       start=True, stop=True),
                                   reads=["onescol_b", accr], writes=[rres],
                                   tag=f"zred{h}_{qhalf}")
                        for qhalf in range(2):
                            rres = f"ps_r{j}{'LH'[qhalf]}"
                            sc.add("dve", lambda e, qhalf=qhalf, rr=rr:
                                   e.reciprocal(
                                       rz[0:1, qhalf * 512:(qhalf + 1) * 512],
                                       rr[0:1, qhalf * 512:(qhalf + 1) * 512]),
                                   reads=[rres], writes=["rz"],
                                   tag=f"rcp{h}_{qhalf}")

                    def tail_b():
                        for qhalf in range(2):
                            rres = f"ps_r{j}{'LH'[qhalf]}"
                            sc.add("pe", lambda e, qhalf=qhalf, rr=rr:
                                   e.matmul(
                                       rr[:, qhalf * 512:(qhalf + 1) * 512],
                                       onesrow_f[:, :],
                                       rz[0:1, qhalf * 512:(qhalf + 1) * 512],
                                       start=True, stop=True),
                                   reads=["onesrow_f", "rz"], writes=[rres],
                                   tag=f"bz{h}_{qhalf}")
                            sc.add("dve", lambda e, qhalf=qhalf, rr=rr:
                                   e.tensor_copy(
                                       bzt[:, qhalf * 512:(qhalf + 1) * 512],
                                       rr[:, qhalf * 512:(qhalf + 1) * 512]),
                                   reads=[rres], writes=["bzt"],
                                   tag=f"bzc{h}_{qhalf}")
                        aslot = slot_ap(h)
                        extra = []
                        if h == 0:
                            extra = ["WbA"]
                        if h == 16:
                            extra = ["WbB"]
                        for qhalf in range(2):
                            pavr = f"ps_av{'LH'[qhalf]}"
                            sc.add("dve", lambda e, aslot=aslot, qhalf=qhalf:
                                   e.tensor_tensor(
                                       aslot[0:P,
                                             qhalf * 512:(qhalf + 1) * 512],
                                       ps_av[:, qhalf * 512:(qhalf + 1) * 512],
                                       bzt[:, qhalf * 512:(qhalf + 1) * 512],
                                       ALU.mult),
                                   reads=[pavr, "bzt"],
                                   writes=[f"slot{h}"]
                                   + (extra if qhalf == 0 else []),
                                   tag=f"ao{h}_{qhalf}")
                    return tail_a, tail_b

                tailq.append(make_tail(h, (slotctr[0] - 1) % 3))
                if h + 1 < NH:
                    emit_q_load(h + 1)
            if kvh + 1 < NKV:
                emit_kv_load(kvh + 1)
        while tailq:
            ta, tb = tailq.pop(0)
            ta()
            tb()

        def emit_wt_load(nb, half, dcl_list=None):
            buf = wt_ring[(2 * nb + half) % 4]
            res = f"wt{(2 * nb + half) % 4}"
            for dcl in (dcl_list if dcl_list is not None else range(16)):
                dc = half * 16 + dcl
                sc.dma("act", lambda e, nb=nb, dc=dc, dcl=dcl, buf=buf:
                       e.dma_start(buf[:, dcl, :],
                                   wdq_scr[nb * 256:(nb + 1) * 256,
                                           dc * 128:(dc + 1) * 128],
                                   transpose=True),
                       reads=["wdq_s0", "wdq_s1"]
                       + (["eser"] if E_SERIALIZE else []),
                       writes=[res] + _take_fence(res), group=res,
                       tag=f"wt{nb}_{dc}")

        def emit_wo_load(nb, half):
            buf = wob_ring[(2 * nb + half) % 3]
            res = f"wob{(2 * nb + half) % 3}"
            for dcl in range(16):
                dc = half * 16 + dcl
                sc.dma("pool", lambda e, nb=nb, dc=dc, dcl=dcl, buf=buf:
                       e.dma_start(buf[:, dcl, :],
                                   wo[dc * 128:(dc + 1) * 128,
                                      nb * 256:(nb + 1) * 256]),
                       reads=(["eser"] if E_SERIALIZE else []),
                       writes=[res] + _take_fence(res), group=res,
                       tag=f"wol{nb}_{dc}")

        _zone_fence = {
            "wt0": ["Kg0"],
            "wt1": ["Kg0", "Kg1"],
            "wt2": ["Kg1", "Vg0", "Vg1"],
            "wt3": ["Vg0", "Vg1", "Vg2", "Vg3", "Vgm", "Qb0"],
            "wob0": [f"pT{j}" for j in range(4)] + [f"ev{j}" for j in range(6)],
            "wob1": ["accZ0", "accZ1", "Qb1", "pT4"],
            "wob2": ["masks"],
        }

        def _take_fence(res):
            return _zone_fence.pop(res, [])
        for half in range(2):
            emit_wt_load(0, half)
            emit_wo_load(0, half)
        for nb in range(16):
            if nb > 0:
                emit_wo_load(nb, 1)
            sc.dma("pool", lambda e, nb=nb: e.dma_start(
                       bias_sb[0:1, nb % 2, :],
                       fbias[0:1, nb * 256:(nb + 1) * 256]),
                   writes=[f"bias{nb % 2}"], tag=f"bld{nb}")
            if DEBUG_TAPS and nb == 9:
                sc.dma("sp", lambda e: e.dma_start(
                           dbg_wob,
                           abf[:, 8192:16384]),
                       reads=["wob0", "wob1"], writes=["dbg"], tag="dwob")
                sc.dma("sp", lambda e: e.dma_start(
                           dbg_wt,
                           abf[:, 0:8192]),
                       reads=["wt0", "wt1"], writes=["dbg"], tag="dwt")
            g_ring = [(ps_r[0][:, 0:256], "ps_r0L"),
                      (ps_r[0][:, 512:768], "ps_r0H"),
                      (ps_r[2][:, 0:256], "ps_r2L")]
            s_ring = [(ps_r[1][:, 0:256], "ps_r1L"),
                      (ps_r[1][:, 512:768], "ps_r1H"),
                      (ps_r[2][:, 512:768], "ps_r2H")]
            for qc in range(NTC):
                pg, pgr = g_ring[qc % 3]
                psn, psr = s_ring[qc % 3]
                xp, xpr = xpc[qc % 2], f"xpc{qc % 2}"
                op_, opr = opc[qc % 2], f"opc{qc % 2}"
                xe_extra = _zone_fence.pop("af32", (["bzt", "rz"]
                           if (nb, qc) == (0, 0) else []))
                sc.dma("sp", lambda e, nb=nb, qc=qc, xp=xp: e.dma_start(
                           xp, x_my[qc * 128:(qc + 1) * 128,
                                    nb * 256:(nb + 1) * 256]),
                       writes=[xpr] + xe_extra, tag=f"xe{nb}_{qc}")
                for dc in range(32):
                    jr = (2 * nb + dc // 16) % 4
                    buf = wt_ring[jr]
                    sc.add("pe", lambda e, dc=dc, qc=qc, pg=pg, buf=buf:
                           e.matmul(pg,
                                    xnT[:, dc, qc * 128:(qc + 1) * 128],
                                    buf[:, dc % 16, :],
                                    start=(dc == 0), stop=False),
                           reads=[f"xnT{dc}", f"wt{jr}"], writes=[pgr],
                           tag=f"g{nb}_{qc}_{dc}")
                sc.add("pe", lambda e, nb=nb, pg=pg: e.matmul(
                           pg, onesrow_b[:, :],
                           bias_sb[0:1, nb % 2, :],
                           start=False, stop=True),
                       reads=["onesrow_b", f"bias{nb % 2}"], writes=[pgr],
                       tag=f"gb{nb}_{qc}")
                sc.add("act", lambda e, pg=pg: e.activation(
                           gpc, pg, AF.Silu),
                       reads=[pgr], writes=["gpc"], tag=f"silu{nb}_{qc}")
                for dc in range(32):
                    jr = (2 * nb + dc // 16) % 3
                    buf = wob_ring[jr]
                    sc.add("pe", lambda e, dc=dc, qc=qc, psn=psn, buf=buf:
                           e.matmul(psn,
                                    slot_ap(dc)[0:P, qc * 128:(qc + 1) * 128],
                                    buf[:, dc % 16, :],
                                    start=(dc == 0), stop=(dc == 31)),
                           reads=[f"slot{dc}", f"wob{jr}"], writes=[psr],
                           tag=f"s{nb}_{qc}_{dc}")
                sc.add("dve", lambda e, psn=psn: e.tensor_copy(spc, psn),
                       reads=[psr], writes=["spc"], tag=f"se{nb}_{qc}")
                if DEBUG_TAPS:
                    sc.dma("sp", lambda e, nb=nb, qc=qc: e.dma_start(
                               dbg_s[qc * 128:(qc + 1) * 128,
                                     nb * 256:(nb + 1) * 256], spc),
                           reads=["spc"], writes=["dbg"], group="dbg")
                    sc.dma("sp", lambda e, nb=nb, qc=qc: e.dma_start(
                               dbg_g[qc * 128:(qc + 1) * 128,
                                     nb * 256:(nb + 1) * 256], gpc),
                           reads=["gpc"], writes=["dbg"], group="dbg")
                sc.add("dve", lambda e: e.tensor_tensor(spc, spc, gpc, ALU.mult),
                       reads=["spc", "gpc"], writes=["spc"], tag=f"sg{nb}_{qc}")
                sc.add("dve", lambda e, op_=op_, xp=xp:
                       e.tensor_tensor(op_, spc, xp, ALU.add),
                       reads=["spc", xpr], writes=[opr], tag=f"o{nb}_{qc}")
                sc.dma("sp", lambda e, nb=nb, qc=qc, op_=op_: e.dma_start(
                           out_my[qc * 128:(qc + 1) * 128,
                                  nb * 256:(nb + 1) * 256], op_),
                       reads=[opr], writes=[f"out_s{qc % 2}"],
                       group=f"out_s{qc % 2}", tag=f"oe{nb}_{qc}")
                if nb + 1 < 16:
                    hf, base = (0, 0) if qc < 4 else (1, -4)
                    emit_wt_load(nb + 1, hf,
                                 dcl_list=range(4 * (qc + base),
                                                4 * (qc + base) + 4))
                    if qc == 0:
                        emit_wo_load(nb + 1, 0)
                if E_SERIALIZE and qc == NTC - 1:
                    sc.add("dve", lambda e: e.engine_nop(),
                           reads=[opr], writes=["eser"], tag=f"eser{nb}")

        if DEBUG_TAPS:
            sc.dma("sp", lambda e: e.dma_start(dbg_xn, xn_scr),
                   reads=["xn_s0", "xn_s1"], writes=["dbg"], group="dbg")
            sc.dma("sp", lambda e: e.dma_start(dbg_q, q_scr),
                   reads=['q_s0', 'q_s1', 'q_s2', 'q_s3', 'q_s4', 'q_s5'], writes=["dbg"], group="dbg")
            sc.dma("sp", lambda e: e.dma_start(dbg_kv, kv_cin.ap()),
                   reads=['kv_s0', 'kv_s1', 'kv_s2', 'kv_s3', 'kv_s4', 'kv_s5'], writes=["dbg"], group="dbg")
            sc.dma("sp", lambda e: e.dma_start(dbg_kvo, kv_cout.ap()),
                   reads=[], writes=["dbg"], waits=[(csem, 1)])
            sc.dma("sp", lambda e: e.dma_start(dbg_wdq, wdq_scr),
                   reads=["wdq_s0", "wdq_s1"], writes=["dbg"], group="dbg")
            sc.dma("sp", lambda e: e.dma_start(dbg_ao, aou[:, :]),
                   reads=[f"slot{h}" for h in range(32)], writes=["dbg"], group="dbg")
            sc.add("sp", lambda e: e.nop(),
                   reads=["out_s0", "out_s1", "dbg"], tag="fin")
        else:
            sc.add("sp", lambda e: e.nop(), reads=["out_s0", "out_s1"],
               tag="fin")
        sc.emit()
        csem_cm.__exit__(None, None, None)
    return nc


def _host_tables():
    btabs = []
    for i in range(4):
        col = np.zeros(24, dtype=np.float32)
        for sl in LO_SLOTS:
            if sl["kind"] == "g" and not (sl["c128"] // 4 < i):
                col[sl["bcol"]] = NEG
        for sl in HI_SLOTS:
            if sl["kind"] == "g" and not (sl["c128"] // 4 < 7 - i):
                col[sl["bcol"]] = NEG
        btabs.append(np.broadcast_to(col, (128, 24)).copy())
    p = np.arange(128)[:, None]
    c = np.arange(512)[None, :]
    m4 = [(r * 128 + p <= c) for r in range(4)]
    masks = np.stack([np.concatenate([m4[2 * pr], m4[2 * pr + 1]], axis=1)
                      for pr in range(2)], axis=0)
    masks = masks.astype(ml_dtypes.bfloat16)
    masks = np.ascontiguousarray(masks.transpose(1, 0, 2))
    return btabs, masks


_NC_CACHE = {}


def make_in_maps(x, wq, wk, wv, wo, wqnt, qsc, qzp, fb, btabs, masks):
    in_maps = []
    for core in range(NCORE):
        b, i = core // 4, core % 4
        lo = x[b, 512 * i:512 * i + 512]
        hi = x[b, 512 * (7 - i):512 * (7 - i) + 512]
        x_my = np.ascontiguousarray(np.concatenate([lo, hi], axis=0))
        in_maps.append(dict(
            x_my=x_my, wq=wq, wk=wk, wv=wv, wo=wo, wqnt=wqnt,
            qsc=qsc, qzp=qzp, fbias=fb, btab_in=btabs[i], masks_in=masks,
        ))
    return in_maps


def kernel(L0_attn_x, L0_attn_wq, L0_attn_wk, L0_attn_wv, L0_attn_wo,
           L0_ffn_weight_q, L0_ffn_scale, L0_ffn_zp, L0_ffn_bias):
    x = np.ascontiguousarray(np.asarray(L0_attn_x, dtype=np.float32))
    wq = np.ascontiguousarray(np.asarray(L0_attn_wq, dtype=np.float32))
    wk = np.ascontiguousarray(np.asarray(L0_attn_wk, dtype=np.float32))
    wv = np.ascontiguousarray(np.asarray(L0_attn_wv, dtype=np.float32))
    wo = np.ascontiguousarray(np.asarray(L0_attn_wo, dtype=np.float32))
    wqnt = np.ascontiguousarray(np.asarray(L0_ffn_weight_q, dtype=np.int32))
    qsc = np.ascontiguousarray(np.asarray(L0_ffn_scale, dtype=np.float32))
    qzp = np.ascontiguousarray(np.asarray(L0_ffn_zp, dtype=np.float32))
    fb = np.ascontiguousarray(
        np.asarray(L0_ffn_bias, dtype=np.float32).reshape(1, D))

    btabs, masks = _host_tables()
    if "nc" not in _NC_CACHE:
        _NC_CACHE["nc"] = build_program()
    nc = _NC_CACHE["nc"]
    in_maps = make_in_maps(x, wq, wk, wv, wo, wqnt, qsc, qzp, fb,
                           btabs, masks)
    res = run_bass_kernel_spmd(nc, in_maps, core_ids=list(range(NCORE)))
    out = np.empty((B, S, D), dtype=np.float32)
    for core in range(NCORE):
        b, i = core // 4, core % 4
        o = res.results[core]["out_my"]
        out[b, 512 * i:512 * i + 512] = o[0:512]
        out[b, 512 * (7 - i):512 * (7 - i) + 512] = o[512:1024]
    return out


# version 33
# speedup vs baseline: 1.0038x; 1.0038x over previous
import numpy as np
import ml_dtypes
import concourse.bass as bass
import concourse.mybir as mybir
from concourse.bass_utils import run_bass_kernel_spmd

F32 = mybir.dt.float32
BF16 = mybir.dt.bfloat16
I32 = mybir.dt.int32
AF = mybir.ActivationFunctionType
ALU = mybir.AluOpType

NCORE = 8
B, S, D = 2, 4096, 4096
NH, NKV, HD = 32, 8, 128
SCALE = float(HD) ** -0.5
EPS = 1e-5
NEG = -80.0
TPC = 1024
NTC = 8

ENGINES = ("pe", "act", "dve", "pool", "sp")
DEBUG_TAPS = False
E_SERIALIZE = False


class _Op:
    __slots__ = ("eng", "emit", "reads", "writes", "is_dma", "sem_val",
                 "dma_val", "dma_key", "tag", "deps", "waits", "group")

    def __init__(self, eng, emit, reads, writes, is_dma, tag, waits, group):
        self.eng, self.emit = eng, emit
        self.reads, self.writes = list(reads), list(writes)
        self.is_dma, self.tag, self.waits = is_dma, tag, list(waits)
        self.sem_val = None
        self.dma_val = None
        self.dma_key = None
        self.group = group


class Sched:

    def __init__(self, nc):
        self.nc = nc
        self.ops = {e: [] for e in ENGINES}
        self.res = {}

    def add(self, eng, emit, reads=(), writes=(), is_dma=False, tag="",
            waits=(), group=None):
        op = _Op(eng, emit, reads, writes, is_dma, tag, waits, group)
        excl = [r for r in op.reads if r.startswith("ps")]
        if excl:
            op.reads = [r for r in op.reads if not r.startswith("ps")]
            op.writes = op.writes + [r for r in excl if r not in op.writes]
        self.ops[eng].append(op)
        deps = []
        for r in op.reads:
            w, _ = self.res.get(r, (None, []))
            if w is not None:
                deps.append(w)
        for r in op.writes:
            w, readers = self.res.get(r, (None, []))
            if w is not None and not (
                    op.is_dma and w.is_dma and w.eng == op.eng
                    and op.group is not None and w.group == op.group):
                deps.append(w)
            deps.extend(readers)
        seen = set()
        op.deps = []
        for d in deps:
            if d is op or id(d) in seen:
                continue
            seen.add(id(d))
            op.deps.append(d)
        for r in op.reads:
            w, readers = self.res.get(r, (None, []))
            readers = [x for x in readers if x.eng != op.eng] + [op]
            self.res[r] = (w, readers)
        for r in op.writes:
            self.res[r] = (op, [])
        return op

    def dma(self, eng, emit, reads=(), writes=(), tag="", waits=(),
            group=None):
        return self.add(eng, emit, reads, writes, is_dma=True, tag=tag,
                        waits=waits, group=group)

    def emit(self):
        nc = self.nc
        needed = set()
        for e in ENGINES:
            for op in self.ops[e]:
                for d in op.deps:
                    if d.eng == op.eng == "pe" and not d.is_dma and not op.is_dma:
                        continue
                    if d.is_dma:
                        continue
                    needed.add(id(d))
        eng_counts = {e: 0 for e in ENGINES}
        dma_counts = {}
        for e in ENGINES:
            for op in self.ops[e]:
                if op.is_dma:
                    key = op.group or (op.writes[0] if op.writes else "_misc")
                    op.dma_key = key
                    dma_counts[key] = dma_counts.get(key, 0) + 16
                    op.dma_val = dma_counts[key]
                if id(op) in needed:
                    eng_counts[e] += 1
                    op.sem_val = eng_counts[e]
        sems, dsems, ctxs = {}, {}, []
        for e in ENGINES:
            cm = nc.semaphore(f"sem_{e}")
            sems[e] = cm.__enter__()
            ctxs.append(cm)
        for key in dma_counts:
            cm = nc.semaphore(f"dsem_{key}")
            dsems[key] = cm.__enter__()
            ctxs.append(cm)
        block_cm = nc.Block()
        block = block_cm.__enter__()
        ctxs.append(block_cm)

        def emit_engine(engobj, e):
            waited = {}
            for op in self.ops[e]:
                for d in op.deps:
                    if d.eng == e == "pe" and not d.is_dma and not op.is_dma:
                        continue
                    if d.is_dma:
                        key = ("d", d.dma_key)
                        if waited.get(key, 0) < d.dma_val:
                            engobj.wait_ge(dsems[d.dma_key], d.dma_val)
                            waited[key] = d.dma_val
                    else:
                        if waited.get(d.eng, 0) < d.sem_val:
                            engobj.wait_ge(sems[d.eng], d.sem_val)
                            waited[d.eng] = d.sem_val
                for (sem, val) in op.waits:
                    key = ("x", sem.name)
                    if waited.get(key, 0) < val:
                        engobj.wait_ge(sem, val)
                        waited[key] = val
                ins = op.emit(engobj)
                if op.is_dma:
                    ins.then_inc(dsems[op.dma_key], 16)
                elif op.sem_val is not None:
                    ins.then_inc(sems[e], 1)

        section = {"pe": block.tensor, "act": block.scalar,
                   "dve": block.vector, "pool": block.gpsimd,
                   "sp": block.sync}
        for e in ENGINES:
            if not self.ops[e]:
                continue
            def make(e):
                def body(engobj):
                    emit_engine(engobj, e)
                return body
            section[e](make(e))
        for c in reversed(ctxs):
            c.__exit__(None, None, None)


def _chunk_owner(c):
    return (c, 0) if c <= 3 else (7 - c, 1)


def _kg_col(c128):
    c, sub = c128 // 4, c128 % 4
    own, half = _chunk_owner(c)
    return own * 1024 + half * 512 + sub * 128


def _v_tile(c128):
    c, sub = c128 // 4, c128 % 4
    own, half = _chunk_owner(c)
    return own * 8 + half * 4 + sub


def _slot_tables():
    lo = []
    for p in range(6):
        lo.append(dict(kind="g", c128=2 * p, bcol=p))
    for pr in range(2):
        lo.append(dict(kind="d", sub=2 * pr, bcol=6 + pr))
    hi = []
    for p in range(14):
        hi.append(dict(kind="g", c128=2 * p, bcol=8 + p))
    for pr in range(2):
        hi.append(dict(kind="d", sub=2 * pr, bcol=22 + pr))
    return lo, hi


LO_SLOTS, HI_SLOTS = _slot_tables()


def build_program():
    nc = bass.Bass("TRN2", target_bir_lowering=False)
    P = 128

    x_my = nc.dram_tensor("x_my", [TPC, D], F32, kind="ExternalInput").ap()
    wq = nc.dram_tensor("wq", [D, D], F32, kind="ExternalInput").ap()
    wk = nc.dram_tensor("wk", [D, 1024], F32, kind="ExternalInput").ap()
    wv = nc.dram_tensor("wv", [D, 1024], F32, kind="ExternalInput").ap()
    wo = nc.dram_tensor("wo", [D, D], F32, kind="ExternalInput").ap()
    wqnt = nc.dram_tensor("wqnt", [D, D], I32, kind="ExternalInput").ap()
    qsc = nc.dram_tensor("qsc", [D, 64], F32, kind="ExternalInput").ap()
    qzp = nc.dram_tensor("qzp", [D, 64], F32, kind="ExternalInput").ap()
    fbias = nc.dram_tensor("fbias", [1, D], F32, kind="ExternalInput").ap()
    btab_in = nc.dram_tensor("btab_in", [P, 24], F32, kind="ExternalInput").ap()
    masks_in = nc.dram_tensor("masks_in", [P, 2, 1024], BF16,
                              kind="ExternalInput").ap()
    out_my = nc.dram_tensor("out_my", [TPC, D], F32, kind="ExternalOutput").ap()
    if DEBUG_TAPS:
        dbg_xn = nc.dram_tensor("dbg_xn", [TPC, D], BF16, kind="ExternalOutput").ap()
        dbg_q = nc.dram_tensor("dbg_q", [TPC, D], BF16, kind="ExternalOutput").ap()
        dbg_kv = nc.dram_tensor("dbg_kv", [2048, 1024], BF16, kind="ExternalOutput").ap()
        dbg_kvo = nc.dram_tensor("dbg_kvo", [8192, 1024], BF16, kind="ExternalOutput").ap()
        dbg_wdq = nc.dram_tensor("dbg_wdq", [D, D], BF16, kind="ExternalOutput").ap()
        dbg_ao = nc.dram_tensor("dbg_ao", [128, 32768], BF16, kind="ExternalOutput").ap()
        dbg_s = nc.dram_tensor("dbg_s", [TPC, D], F32, kind="ExternalOutput").ap()
        dbg_g = nc.dram_tensor("dbg_g", [TPC, D], F32, kind="ExternalOutput").ap()
        dbg_wob = nc.dram_tensor("dbg_wob", [128, 8192], BF16, kind="ExternalOutput").ap()
        dbg_wt = nc.dram_tensor("dbg_wt", [128, 8192], BF16, kind="ExternalOutput").ap()

    xn_scr = nc.dram_tensor("xn_scr", [TPC, D], BF16).ap()
    q_scr = nc.dram_tensor("q_scr", [TPC, D], BF16).ap()
    kv_cin = nc.dram_tensor("kv_cin", [2048, 1024], BF16)
    kv_cout = nc.dram_tensor("kv_cout", [8192, 1024], BF16)
    wdq_scr = nc.dram_tensor("wdq_scr", [D, D], BF16).ap()
    wdqT_scr = nc.dram_tensor("wdqT_scr", [D, D], BF16).ap()

    from contextlib import ExitStack
    with ExitStack() as ctx:
        def sb(name, shape, dt):
            return ctx.enter_context(nc.sbuf_tensor(name, shape, dt))

        def psum(name, shape):
            return ctx.enter_context(nc.psum_tensor(name, shape, F32))

        xnT = sb("xnT", [P, 32, 1024], BF16)
        aou = sb("aou", [P, 32768], BF16)
        masks = None
        btab = sb("btab", [P, 24], F32)
        onescol = sb("onescol", [P, 1], F32)
        onescol_b = sb("onescol_b", [P, 1], BF16)
        onesrow_f = sb("onesrow_f", [1, P], F32)
        onesrow_b = sb("onesrow_b", [1, P], BF16)
        epscol = sb("epscol", [P, 1], F32)
        c_sum = sb("c_sum", [P, 2], F32)
        c_sq = sb("c_sq", [P, 2], F32)
        c_m = sb("c_m", [P, 2], F32)
        c_var = sb("c_var", [P, 2], F32)
        c_m2 = sb("c_m2", [P, 2], F32)
        c_nr = sb("c_nr", [P, 2], F32)
        c_rstd = sb("c_rstd", [P, 2], F32)
        bias_sb = sb("bias_sb", [1, 2, 256], BF16)

        abf = sb("abf", [P, 28672], BF16)
        af32 = sb("af32", [P, 2048], F32)
        d_qf = sb("d_qf", [P, 2, 1024], F32)
        d_out = sb("d_out", [P, 2, 1024], BF16)
        d_zp = sb("d_zp", [P, 1, 64], F32)
        wtt = sb("wtt", [P, 2, 256], BF16)
        d_sc = sb("d_sc", [P, 1, 64], F32)

        ps_r = [psum("ps_r0", [P, 1024]), psum("ps_r1", [P, 1024]),
                psum("ps_r2", [P, 1024])]
        ps_av = psum("ps_av", [P, 1024])

        csem_cm = nc.semaphore("csem")
        csem = csem_cm.__enter__()

        sc = Sched(nc)

        aou_f32 = aou.bitcast(F32)
        xA = [aou_f32[:, 0:4096], aou_f32[:, 4096:8192]]
        xsq = aou_f32[:, 8192:12288]
        xnb = [aou[:, 24576:28672], aou[:, 28672:32768]]
        WbP = [aou[:, 0:16384].rearrange("p (dc n) -> p dc n", dc=32),
               aou[:, 16384:32768].rearrange("p (dc n) -> p dc n", dc=32)]

        def slot_ap(h):
            return aou[:, h * 1024:(h + 1) * 1024]

        Kg = [abf[:, 0:5120], abf[:, 5120:10240]]
        Vg = abf[:, 10240:15360].rearrange("p (t d) -> p t d", d=128)
        Qb = [abf[:, 15360:16384], abf[:, 22528:23552]]
        pT = [abf[:, 16384 + 1024 * j:17408 + 1024 * j] for j in range(4)]
        pT.append(abf[:, 23552:24576])
        masks = abf[:, 24576:26624].rearrange("p (a c) -> p a c", a=2)
        accZ2 = [abf[:, 20480:21504], abf[:, 21504:22528]]
        wt_ring = [abf[:, 4096 * j:4096 * (j + 1)]
                   .rearrange("p (dc n) -> p dc n", dc=16) for j in range(4)]
        wob_ring = [abf[:, 16384 + 4096 * j:16384 + 4096 * (j + 1)]
                    .rearrange("p (dc n) -> p dc n", dc=16) for j in range(3)]

        bzt = af32[:, 0:1024]
        rz = af32[0:1, 1024:2048]
        xpc = [af32[:, 0:256], af32[:, 256:512]]
        spc = af32[:, 512:768]
        gpc = af32[:, 768:1024]
        opc = [af32[:, 1024:1280], af32[:, 1280:1536]]

        sc.add("dve", lambda e: e.memset(onescol[:, :], 1.0), writes=["onescol"])
        sc.add("dve", lambda e: e.memset(onescol_b[:, :], 1.0), writes=["onescol_b"])
        sc.add("dve", lambda e: e.memset(onesrow_f[:, :], 1.0), writes=["onesrow_f"])
        sc.add("dve", lambda e: e.memset(onesrow_b[:, :], 1.0), writes=["onesrow_b"])
        sc.add("dve", lambda e: e.memset(epscol[:, :], EPS), writes=["epscol"])
        sc.dma("act", lambda e: e.dma_start(btab[:, :], btab_in), writes=["btab"])
        sc.dma("act", lambda e: e.dma_start(masks[:, :, :], masks_in),
               writes=["masks"])

        for tc in range(NTC):
            p = tc % 2
            xa, xr = xA[p], f"xA{p}"
            cs = lambda t, p=p: t[:, p:p + 1]
            sfx = str(p)
            sc.dma("sp", lambda e, tc=tc, xa=xa:
                   e.dma_start(xa, x_my[tc * 128:(tc + 1) * 128, :]),
                   writes=[xr], tag=f"ldx{tc}")
            xb, xbr = xnb[p], f"xnb{p}"
            sc.add("act", lambda e, xa=xa, xb=xb, p=p: e.activation(
                       xb, xa, AF.Copy, accum_out=c_sum[:, p:p + 1]),
                   reads=[xr], writes=[xbr, "c_sum" + sfx])
            sc.add("act", lambda e, xa=xa, p=p: e.activation(
                       xsq, xa, AF.Square, accum_out=c_sq[:, p:p + 1]),
                   reads=[xr], writes=["xsq", "c_sq" + sfx])
            sc.add("dve", lambda e, p=p: e.tensor_scalar(
                       c_m[:, p:p + 1], c_sum[:, p:p + 1], 1.0 / D, None,
                       ALU.mult),
                   reads=["c_sum" + sfx], writes=["c_m" + sfx])
            sc.add("dve", lambda e, p=p: e.tensor_scalar(
                       c_var[:, p:p + 1], c_sq[:, p:p + 1], 1.0 / D, None,
                       ALU.mult),
                   reads=["c_sq" + sfx], writes=["c_var" + sfx])
            sc.add("dve", lambda e, p=p: e.tensor_tensor(
                       c_m2[:, p:p + 1], c_m[:, p:p + 1], c_m[:, p:p + 1],
                       ALU.mult),
                   reads=["c_m" + sfx], writes=["c_m2" + sfx])
            sc.add("dve", lambda e, p=p: e.tensor_tensor(
                       c_var[:, p:p + 1], c_var[:, p:p + 1], c_m2[:, p:p + 1],
                       ALU.subtract),
                   reads=["c_var" + sfx, "c_m2" + sfx], writes=["c_var" + sfx],
                   tag="var")
            sc.add("dve", lambda e, p=p: e.tensor_scalar(
                       c_var[:, p:p + 1], c_var[:, p:p + 1], EPS, None,
                       ALU.add),
                   reads=["c_var" + sfx], writes=["c_var" + sfx])
            sc.add("act", lambda e, p=p: e.activation(
                       c_rstd[:, p:p + 1], c_var[:, p:p + 1], AF.Sqrt),
                   reads=["c_var" + sfx], writes=["c_rstd" + sfx])
            sc.add("dve", lambda e, p=p: e.reciprocal(
                       c_rstd[:, p:p + 1], c_rstd[:, p:p + 1]),
                   reads=["c_rstd" + sfx], writes=["c_rstd" + sfx])
            sc.add("dve", lambda e, p=p: e.tensor_tensor(
                       c_nr[:, p:p + 1], c_rstd[:, p:p + 1],
                       c_rstd[:, p:p + 1], ALU.mult),
                   reads=["c_rstd" + sfx], writes=["c_nr" + sfx])
            sc.add("dve", lambda e, p=p: e.tensor_tensor(
                       c_nr[:, p:p + 1], c_nr[:, p:p + 1], c_var[:, p:p + 1],
                       ALU.mult),
                   reads=["c_nr" + sfx, "c_var" + sfx], writes=["c_nr" + sfx])
            sc.add("dve", lambda e, p=p: e.tensor_scalar(
                       c_nr[:, p:p + 1], c_nr[:, p:p + 1], -0.5, 1.5,
                       ALU.mult, ALU.add),
                   reads=["c_nr" + sfx], writes=["c_nr" + sfx])
            sc.add("dve", lambda e, p=p: e.tensor_tensor(
                       c_rstd[:, p:p + 1], c_rstd[:, p:p + 1], c_nr[:, p:p + 1],
                       ALU.mult),
                   reads=["c_rstd" + sfx, "c_nr" + sfx],
                   writes=["c_rstd" + sfx])
            sc.add("dve", lambda e, xa=xa, xb=xb, p=p: e.tensor_scalar(
                       xb, xa, c_m[:, p:p + 1], c_rstd[:, p:p + 1],
                       ALU.subtract, ALU.mult),
                   reads=[xr, "c_m" + sfx, "c_rstd" + sfx], writes=[xbr],
                   tag=f"xn{tc}")
            sc.dma("act", lambda e, tc=tc, xb=xb:
                   e.dma_start(xn_scr[tc * 128:(tc + 1) * 128, :], xb),
                   reads=[xbr], writes=[f"xn_s{tc % 2}"],
                   group=f"xn_s{tc % 2}", tag=f"stxn{tc}")
        for dc in range(32):
            sc.dma("act", lambda e, dc=dc: e.dma_start(
                       xnT[:, dc, :], xn_scr[:, dc * 128:(dc + 1) * 128],
                       transpose=True),
                   reads=["xn_s0", "xn_s1"], writes=[f"xnT{dc}"],
                   tag=f"trxn{dc}")

        def emit_deq(nb):
            sc.dma("pool", lambda e, nb=nb: e.dma_start(
                       d_zp[:, 0, :], qzp[nb * 128:(nb + 1) * 128, :]),
                   writes=["dzp"], tag=f"dzp{nb}")
            sc.dma("pool", lambda e, nb=nb: e.dma_start(
                       d_sc[:, 0, :], qsc[nb * 128:(nb + 1) * 128, :]),
                   writes=["dsc"], tag=f"dsc{nb}")
            for qq in range(4):
                p = qq % 2
                sc.dma("pool", lambda e, nb=nb, qq=qq, p=p: e.dma_start(
                           d_qf[:, p, :],
                           wqnt[nb * 128:(nb + 1) * 128,
                                qq * 1024:(qq + 1) * 1024]),
                       writes=[f"dI{p}"], tag=f"dq{nb}_{qq}")
                g0 = qq * 16
                zpb = d_zp[:, 0, :].rearrange("p (g u) -> p g u", u=1)[
                    :, g0:g0 + 16, :].broadcast_to([P, 16, 64])
                scb = d_sc[:, 0, :].rearrange("p (g u) -> p g u", u=1)[
                    :, g0:g0 + 16, :].broadcast_to([P, 16, 64])
                qfv = d_qf[:, p, :].rearrange("p (g u) -> p g u", g=16)
                outv = d_out[:, p, :].rearrange("p (g u) -> p g u", g=16)
                sc.add("pool", lambda e, qfv=qfv, zpb=zpb:
                       e.tensor_tensor(qfv, qfv, zpb, ALU.subtract),
                       reads=[f"dI{p}", "dzp"], writes=[f"dI{p}"],
                       tag=f"dqs{nb}_{qq}")
                sc.add("pool", lambda e, outv=outv, qfv=qfv, scb=scb:
                       e.tensor_tensor(outv, qfv, scb, ALU.mult),
                       reads=[f"dI{p}", "dsc"], writes=[f"dO{p}"],
                       tag=f"dqm{nb}_{qq}")
                sc.dma("pool", lambda e, nb=nb, qq=qq, p=p: e.dma_start(
                           wdq_scr[nb * 128:(nb + 1) * 128,
                                   qq * 1024:(qq + 1) * 1024],
                           d_out[:, p, :]),
                       reads=[f"dO{p}"], writes=[f"wdq_s{p}"],
                       group=f"wdq_s{p}", tag=f"dqst{nb}_{qq}")

        def emit_wdqT(n2):
            for dc in range(32):
                p = dc % 2
                sc.dma("sp", lambda e, n2=n2, dc=dc, p=p: e.dma_start(
                           wtt[:, p, :],
                           wdq_scr[n2 * 256:(n2 + 1) * 256,
                                   dc * 128:(dc + 1) * 128],
                           transpose=True),
                       reads=["wdq_s0", "wdq_s1"], writes=[f"wtt{p}"],
                       tag=f"wTt{n2}_{dc}")
                sc.dma("sp", lambda e, n2=n2, dc=dc, p=p: e.dma_start(
                           wdqT_scr[dc * 128:(dc + 1) * 128,
                                    n2 * 256:(n2 + 1) * 256],
                           wtt[:, p, :]),
                       reads=[f"wtt{p}"], writes=[f"wdqT_s{p}"],
                       group=f"wdqT_s{p}", tag=f"wTs{n2}_{dc}")


        pj_blocks = ([("k", wk, bi) for bi in range(2)]
                     + [("v", wv, bi) for bi in range(2)]
                     + [("q", wq, bi) for bi in range(8)])
        ev = [abf[:, 16384 + 512 * j:16896 + 512 * j] for j in range(6)]
        wb_fence = [["xA0", "xA1"], ["xsq", "xnb0", "xnb1"]]
        for bx, (w_kind, w_ap, bi) in enumerate(pj_blocks):
            par = bx % 2
            Wb = WbP[par]
            wn = "AB"[par]
            for dc in range(32):
                extra = wb_fence[par]
                wb_fence[par] = []
                sc.dma("pool", lambda e, dc=dc, bi=bi, w_ap=w_ap, Wb=Wb:
                       e.dma_start(Wb[:, dc, :],
                                   w_ap[dc * 128:(dc + 1) * 128,
                                        bi * 512:(bi + 1) * 512]),
                       reads=[], writes=[f"Wb{wn}"] + extra,
                       group=f"Wb{wn}", tag=f"ldw{bx}_{dc}")
            for tc in range(NTC):
                rj, rh = (tc % 6) // 2, (tc % 6) % 2
                pp = ps_r[rj][:, rh * 512:(rh + 1) * 512]
                ppr = f"ps_r{rj}{'LH'[rh]}"
                for dc in range(32):
                    sc.add("pe", lambda e, dc=dc, tc=tc, pp=pp, Wb=Wb:
                           e.matmul(pp, xnT[:, dc, tc * 128:(tc + 1) * 128],
                                    Wb[:, dc, :], start=(dc == 0),
                                    stop=(dc == 31)),
                           reads=[f"xnT{dc}", f"Wb{wn}"], writes=[ppr],
                           tag=f"pj{bx}_{tc}_{dc}")
                evb, evr = ev[tc % 6], f"ev{tc % 6}"
                sc.add("act", lambda e, evb=evb, pp=pp:
                       e.copy(evb, pp),
                       reads=[ppr], writes=[evr], tag=f"evac{bx}_{tc}")
                if w_kind == "k":
                    dst = kv_cin.ap()[tc * 128:(tc + 1) * 128,
                                      bi * 512:(bi + 1) * 512]
                    dres = f"kv_s{tc % 6}"
                elif w_kind == "v":
                    dst = kv_cin.ap()[1024 + tc * 128:1024 + (tc + 1) * 128,
                                      bi * 512:(bi + 1) * 512]
                    dres = f"kv_s{tc % 6}"
                else:
                    dst = q_scr[tc * 128:(tc + 1) * 128,
                                bi * 512:(bi + 1) * 512]
                    dres = f"q_s{tc % 6}"
                sc.dma("act", lambda e, dst=dst, evb=evb: e.dma_start(dst, evb),
                       reads=[evr], writes=[dres], group=dres,
                       tag=f"st{bx}_{tc}")
            if w_kind == "q" and bi == 3:
                def cc(e):
                    return e.collective_compute(
                        "AllGather", ALU.bypass,
                        replica_groups=[[0, 1, 2, 3], [4, 5, 6, 7]],
                        ins=[kv_cin.ap()], outs=[kv_cout.ap()],
                    ).then_inc(csem, 1)
                sc.add("pool", cc, reads=['kv_s0', 'kv_s1', 'kv_s2', 'kv_s3', 'kv_s4', 'kv_s5'], writes=[], tag="cc")

        for nb in range(32):
            emit_deq(nb)

        def emit_kv_load(kvh):
            p = kvh % 2
            kg, kgr = Kg[p], f"Kg{p}"
            w = [(csem, 1)]
            for r in range(4):
                sc.dma("sp", lambda e, r=r, kvh=kvh, kg=kg: e.dma_start(
                           kg[0:P, r * 1024:(r + 1) * 1024],
                           kv_cout.ap()[r * 2048:r * 2048 + 1024,
                                        kvh * 128:(kvh + 1) * 128],
                           transpose=True),
                       reads=[], writes=[kgr], group=kgr, tag=f"kg{kvh}_{r}",
                       waits=w)
            sc.dma("sp", lambda e, kvh=kvh, kg=kg: e.dma_start(
                       kg[0:P, 4096:5120],
                       kv_cin.ap()[0:1024, kvh * 128:(kvh + 1) * 128],
                       transpose=True),
                   reads=['kv_s0', 'kv_s1', 'kv_s2', 'kv_s3', 'kv_s4', 'kv_s5'], writes=[kgr], group=kgr,
                   tag=f"kgm{kvh}")
            for r in range(4):
                sc.dma("sp", lambda e, r=r, kvh=kvh: e.dma_start(
                           Vg[:, r * 8:(r + 1) * 8, :],
                           kv_cout.ap()[r * 2048 + 1024:(r + 1) * 2048,
                                        kvh * 128:(kvh + 1) * 128]
                           .rearrange("(a p) d -> p a d", p=128)),
                       reads=[], writes=[f"Vg{r}"], group=f"Vg{r}",
                       tag=f"vg{kvh}_{r}", waits=w)
            sc.dma("sp", lambda e, kvh=kvh: e.dma_start(
                       Vg[:, 32:40, :],
                       kv_cin.ap()[1024:2048, kvh * 128:(kvh + 1) * 128]
                       .rearrange("(a p) d -> p a d", p=128)),
                   reads=['kv_s0', 'kv_s1', 'kv_s2', 'kv_s3', 'kv_s4', 'kv_s5'],
                   writes=["Vgm"], group="Vgm", tag=f"vgm{kvh}")

        def emit_q_load(h):
            extra = ["ev0", "ev1"] if h == 0 else []
            p = h % 2
            sc.dma("sp", lambda e, h=h, p=p: e.dma_start(
                       Qb[p], q_scr[0:1024, h * 128:(h + 1) * 128],
                       transpose=True),
                   reads=['q_s0', 'q_s1', 'q_s2', 'q_s3', 'q_s4', 'q_s5'],
                   writes=[f"Qb{p}"] + extra, tag=f"qld{h}")

        emit_kv_load(0)
        emit_q_load(0)
        slotctr = [0]
        tailq = []
        ev_fence = [[f"ev{j}" for j in range(6)]]

        def emit_pass(h, kvh, qhalf, pairs, avh, inject=()):
            kp = kvh % 2
            kgr, qr = f"Kg{kp}", f"Qb{h % 2}"
            kg = Kg[kp]
            q = Qb[h % 2]
            n = len(pairs)
            LAG = 4
            avap = ps_av[:, avh * 512:(avh + 1) * 512]
            avr = f"ps_av{'LH'[avh]}"

            def pair_aps(pp_):
                out = []
                for half in range(2):
                    if pp_["kind"] == "g":
                        c128 = pp_["c128"] + half
                        kap = kg[0:P, _kg_col(c128):_kg_col(c128) + 128]
                        vap = Vg[:, _v_tile(c128), :]
                        vres = f"Vg{_v_tile(c128) // 8}"
                    else:
                        sub = pp_["sub"] + half
                        koff = 4096 + qhalf * 512 + sub * 128
                        kap = kg[0:P, koff:koff + 128]
                        vap = Vg[:, 32 + qhalf * 4 + sub, :]
                        vres = "Vgm"
                    out.append((kap, vap, vres))
                return out

            def emit_sc(px):
                pp_ = pairs[px]
                j = slotctr[0] % 3
                jt = slotctr[0] % 5
                rr = ps_r[j]
                pt = pT[jt]
                pair_ps[px] = (j, jt, rr, pt)
                slotctr[0] += 1
                aps = pair_aps(pp_)
                for half in range(2):
                    kap = aps[half][0]
                    sc.add("pe", lambda e, kap=kap, rr=rr, half=half:
                           e.matmul(rr[:, half * 512:(half + 1) * 512], kap,
                                    q[0:P, qhalf * 512:(qhalf + 1) * 512],
                                    start=True, stop=True),
                           reads=[kgr, qr], writes=[f"ps_r{j}{'LH'[half]}"],
                           tag=f"sc{h}_{qhalf}_{px}_{half}")
                pt_extra = ev_fence[0]
                ev_fence[0] = []
                sc.add("act", lambda e, pt=pt, rr=rr, pp_=pp_:
                       e.activation(pt, rr[:, :], AF.Exp,
                                    bias=btab[:, pp_["bcol"]:pp_["bcol"] + 1],
                                    scale=SCALE),
                       reads=[f"ps_r{j}L", f"ps_r{j}H", "btab"],
                       writes=[f"pT{jt}"] + pt_extra,
                       tag=f"exp{h}_{qhalf}_{px}")
                if pp_["kind"] == "d":
                    pr = pp_["sub"] // 2
                    sc.add("dve", lambda e, pt=pt, pr=pr:
                           e.tensor_tensor(pt, pt, masks[:, pr, :], ALU.mult),
                           reads=[f"pT{jt}", "masks"], writes=[f"pT{jt}"],
                           tag=f"msk{h}_{qhalf}_{px}")
                accv = accZ2[h % 2]
                accr = f"accZ{h % 2}"
                for half in range(2):
                    sc.add("dve", lambda e, pt=pt, half=half, accv=accv:
                           e.tensor_tensor(
                               accv[0:P, qhalf * 512:(qhalf + 1) * 512],
                               accv[0:P, qhalf * 512:(qhalf + 1) * 512],
                               pt[0:P, half * 512:(half + 1) * 512], ALU.add),
                           reads=[f"pT{jt}", accr], writes=[accr],
                           tag=f"zac{h}_{qhalf}_{px}_{half}")

            def emit_av(px):
                pp_ = pairs[px]
                j, jt, rr, pt = pair_ps[px]
                aps = pair_aps(pp_)
                for half in range(2):
                    _, vap, vres = aps[half]
                    sx = 2 * px + half
                    sc.add("pe", lambda e, vap=vap, pt=pt, half=half,
                           sx=sx, avap=avap:
                           e.matmul(avap, vap,
                                    pt[0:P, half * 512:(half + 1) * 512],
                                    start=(sx == 0), stop=(sx == 2 * n - 1)),
                           reads=[vres, f"pT{jt}"], writes=[avr],
                           tag=f"av{h}_{qhalf}_{sx}")

            pair_ps = {}
            inj = dict(inject)
            for px in range(n + LAG):
                if px < n:
                    emit_sc(px)
                if px in inj:
                    inj[px]()
                if px >= LAG:
                    emit_av(px - LAG)

        for kvh in range(NKV):
            for qq in range(4):
                h = kvh * 4 + qq
                sc.add("dve", lambda e, h=h: e.memset(accZ2[h % 2][0:P, :], 0.0),
                       reads=[], writes=[f"accZ{h % 2}"], tag=f"zclr{h}")
                inj = []
                if tailq:
                    ta, tb = tailq.pop(0)
                    inj = [(0, ta), (2, tb)]
                emit_pass(h, kvh, 0, LO_SLOTS, 0, inject=inj)
                emit_pass(h, kvh, 1, HI_SLOTS, 1)

                def make_tail(h, j):
                    rr = ps_r[j]
                    accv = accZ2[h % 2]
                    accr = f"accZ{h % 2}"

                    def tail_a():
                        for qhalf in range(2):
                            rres = f"ps_r{j}{'LH'[qhalf]}"
                            sc.add("pe", lambda e, qhalf=qhalf, rr=rr,
                                   accv=accv: e.matmul(
                                       rr[0:1, qhalf * 512:(qhalf + 1) * 512],
                                       onescol_b[:, :],
                                       accv[0:P, qhalf * 512:(qhalf + 1) * 512],
                                       start=True, stop=True),
                                   reads=["onescol_b", accr], writes=[rres],
                                   tag=f"zred{h}_{qhalf}")
                        for qhalf in range(2):
                            rres = f"ps_r{j}{'LH'[qhalf]}"
                            sc.add("dve", lambda e, qhalf=qhalf, rr=rr:
                                   e.reciprocal(
                                       rz[0:1, qhalf * 512:(qhalf + 1) * 512],
                                       rr[0:1, qhalf * 512:(qhalf + 1) * 512]),
                                   reads=[rres], writes=["rz"],
                                   tag=f"rcp{h}_{qhalf}")

                    def tail_b():
                        for qhalf in range(2):
                            rres = f"ps_r{j}{'LH'[qhalf]}"
                            sc.add("pe", lambda e, qhalf=qhalf, rr=rr:
                                   e.matmul(
                                       rr[:, qhalf * 512:(qhalf + 1) * 512],
                                       onesrow_f[:, :],
                                       rz[0:1, qhalf * 512:(qhalf + 1) * 512],
                                       start=True, stop=True),
                                   reads=["onesrow_f", "rz"], writes=[rres],
                                   tag=f"bz{h}_{qhalf}")
                            sc.add("dve", lambda e, qhalf=qhalf, rr=rr:
                                   e.tensor_copy(
                                       bzt[:, qhalf * 512:(qhalf + 1) * 512],
                                       rr[:, qhalf * 512:(qhalf + 1) * 512]),
                                   reads=[rres], writes=["bzt"],
                                   tag=f"bzc{h}_{qhalf}")
                        aslot = slot_ap(h)
                        extra = []
                        if h == 0:
                            extra = ["WbA"]
                        if h == 16:
                            extra = ["WbB"]
                        for qhalf in range(2):
                            pavr = f"ps_av{'LH'[qhalf]}"
                            sc.add("dve", lambda e, aslot=aslot, qhalf=qhalf:
                                   e.tensor_tensor(
                                       aslot[0:P,
                                             qhalf * 512:(qhalf + 1) * 512],
                                       ps_av[:, qhalf * 512:(qhalf + 1) * 512],
                                       bzt[:, qhalf * 512:(qhalf + 1) * 512],
                                       ALU.mult),
                                   reads=[pavr, "bzt"],
                                   writes=[f"slot{h}"]
                                   + (extra if qhalf == 0 else []),
                                   tag=f"ao{h}_{qhalf}")
                    return tail_a, tail_b

                tailq.append(make_tail(h, (slotctr[0] - 1) % 3))
                if h + 1 < NH:
                    emit_q_load(h + 1)
            if kvh + 1 < NKV:
                emit_kv_load(kvh + 1)
        while tailq:
            ta, tb = tailq.pop(0)
            ta()
            tb()

        def emit_wt_load(nb, half, dcl_list=None):
            buf = wt_ring[(2 * nb + half) % 4]
            res = f"wt{(2 * nb + half) % 4}"
            for dcl in (dcl_list if dcl_list is not None else range(16)):
                dc = half * 16 + dcl
                sc.dma("act", lambda e, nb=nb, dc=dc, dcl=dcl, buf=buf:
                       e.dma_start(buf[:, dcl, :],
                                   wdq_scr[nb * 256:(nb + 1) * 256,
                                           dc * 128:(dc + 1) * 128],
                                   transpose=True),
                       reads=["wdq_s0", "wdq_s1"]
                       + (["eser"] if E_SERIALIZE else []),
                       writes=[res] + _take_fence(res), group=res,
                       tag=f"wt{nb}_{dc}")

        def emit_wo_load(nb, half):
            buf = wob_ring[(2 * nb + half) % 3]
            res = f"wob{(2 * nb + half) % 3}"
            for dcl in range(16):
                dc = half * 16 + dcl
                sc.dma("pool", lambda e, nb=nb, dc=dc, dcl=dcl, buf=buf:
                       e.dma_start(buf[:, dcl, :],
                                   wo[dc * 128:(dc + 1) * 128,
                                      nb * 256:(nb + 1) * 256]),
                       reads=(["eser"] if E_SERIALIZE else []),
                       writes=[res] + _take_fence(res), group=res,
                       tag=f"wol{nb}_{dc}")

        _zone_fence = {
            "wt0": ["Kg0"],
            "wt1": ["Kg0", "Kg1"],
            "wt2": ["Kg1", "Vg0", "Vg1"],
            "wt3": ["Vg0", "Vg1", "Vg2", "Vg3", "Vgm", "Qb0"],
            "wob0": [f"pT{j}" for j in range(4)] + [f"ev{j}" for j in range(6)],
            "wob1": ["accZ0", "accZ1", "Qb1", "pT4"],
            "wob2": ["masks"],
        }

        def _take_fence(res):
            return _zone_fence.pop(res, [])
        for half in range(2):
            emit_wt_load(0, half)
            emit_wo_load(0, half)
        for nb in range(16):
            if nb > 0:
                emit_wo_load(nb, 1)
            sc.dma("pool", lambda e, nb=nb: e.dma_start(
                       bias_sb[0:1, nb % 2, :],
                       fbias[0:1, nb * 256:(nb + 1) * 256]),
                   writes=[f"bias{nb % 2}"], tag=f"bld{nb}")
            if DEBUG_TAPS and nb == 9:
                sc.dma("sp", lambda e: e.dma_start(
                           dbg_wob,
                           abf[:, 8192:16384]),
                       reads=["wob0", "wob1"], writes=["dbg"], tag="dwob")
                sc.dma("sp", lambda e: e.dma_start(
                           dbg_wt,
                           abf[:, 0:8192]),
                       reads=["wt0", "wt1"], writes=["dbg"], tag="dwt")
            g_ring = [(ps_r[0][:, 0:256], "ps_r0L"),
                      (ps_r[0][:, 512:768], "ps_r0H"),
                      (ps_r[2][:, 0:256], "ps_r2L")]
            s_ring = [(ps_r[1][:, 0:256], "ps_r1L"),
                      (ps_r[1][:, 512:768], "ps_r1H"),
                      (ps_r[2][:, 512:768], "ps_r2H")]
            for qc in range(NTC):
                pg, pgr = g_ring[qc % 3]
                psn, psr = s_ring[qc % 3]
                xp, xpr = xpc[qc % 2], f"xpc{qc % 2}"
                op_, opr = opc[qc % 2], f"opc{qc % 2}"
                xe_extra = _zone_fence.pop("af32", (["bzt", "rz"]
                           if (nb, qc) == (0, 0) else []))
                sc.dma("sp", lambda e, nb=nb, qc=qc, xp=xp: e.dma_start(
                           xp, x_my[qc * 128:(qc + 1) * 128,
                                    nb * 256:(nb + 1) * 256]),
                       writes=[xpr] + xe_extra, tag=f"xe{nb}_{qc}")
                for dc in range(32):
                    jr = (2 * nb + dc // 16) % 4
                    buf = wt_ring[jr]
                    sc.add("pe", lambda e, dc=dc, qc=qc, pg=pg, buf=buf:
                           e.matmul(pg,
                                    xnT[:, dc, qc * 128:(qc + 1) * 128],
                                    buf[:, dc % 16, :],
                                    start=(dc == 0), stop=False),
                           reads=[f"xnT{dc}", f"wt{jr}"], writes=[pgr],
                           tag=f"g{nb}_{qc}_{dc}")
                sc.add("pe", lambda e, nb=nb, pg=pg: e.matmul(
                           pg, onesrow_b[:, :],
                           bias_sb[0:1, nb % 2, :],
                           start=False, stop=True),
                       reads=["onesrow_b", f"bias{nb % 2}"], writes=[pgr],
                       tag=f"gb{nb}_{qc}")
                sc.add("act", lambda e, pg=pg: e.activation(
                           gpc, pg, AF.Silu),
                       reads=[pgr], writes=["gpc"], tag=f"silu{nb}_{qc}")
                for dc in range(32):
                    jr = (2 * nb + dc // 16) % 3
                    buf = wob_ring[jr]
                    sc.add("pe", lambda e, dc=dc, qc=qc, psn=psn, buf=buf:
                           e.matmul(psn,
                                    slot_ap(dc)[0:P, qc * 128:(qc + 1) * 128],
                                    buf[:, dc % 16, :],
                                    start=(dc == 0), stop=(dc == 31)),
                           reads=[f"slot{dc}", f"wob{jr}"], writes=[psr],
                           tag=f"s{nb}_{qc}_{dc}")
                sc.add("dve", lambda e, psn=psn: e.tensor_copy(spc, psn),
                       reads=[psr], writes=["spc"], tag=f"se{nb}_{qc}")
                if DEBUG_TAPS:
                    sc.dma("sp", lambda e, nb=nb, qc=qc: e.dma_start(
                               dbg_s[qc * 128:(qc + 1) * 128,
                                     nb * 256:(nb + 1) * 256], spc),
                           reads=["spc"], writes=["dbg"], group="dbg")
                    sc.dma("sp", lambda e, nb=nb, qc=qc: e.dma_start(
                               dbg_g[qc * 128:(qc + 1) * 128,
                                     nb * 256:(nb + 1) * 256], gpc),
                           reads=["gpc"], writes=["dbg"], group="dbg")
                sc.add("dve", lambda e: e.tensor_tensor(spc, spc, gpc, ALU.mult),
                       reads=["spc", "gpc"], writes=["spc"], tag=f"sg{nb}_{qc}")
                sc.add("dve", lambda e, op_=op_, xp=xp:
                       e.tensor_tensor(op_, spc, xp, ALU.add),
                       reads=["spc", xpr], writes=[opr], tag=f"o{nb}_{qc}")
                sc.dma("sp", lambda e, nb=nb, qc=qc, op_=op_: e.dma_start(
                           out_my[qc * 128:(qc + 1) * 128,
                                  nb * 256:(nb + 1) * 256], op_),
                       reads=[opr], writes=[f"out_s{qc % 2}"],
                       group=f"out_s{qc % 2}", tag=f"oe{nb}_{qc}")
                if nb + 1 < 16:
                    hf, base = (0, 0) if qc < 4 else (1, -4)
                    emit_wt_load(nb + 1, hf,
                                 dcl_list=range(4 * (qc + base),
                                                4 * (qc + base) + 4))
                    if qc == 0:
                        emit_wo_load(nb + 1, 0)
                if E_SERIALIZE and qc == NTC - 1:
                    sc.add("dve", lambda e: e.engine_nop(),
                           reads=[opr], writes=["eser"], tag=f"eser{nb}")

        if DEBUG_TAPS:
            sc.dma("sp", lambda e: e.dma_start(dbg_xn, xn_scr),
                   reads=["xn_s0", "xn_s1"], writes=["dbg"], group="dbg")
            sc.dma("sp", lambda e: e.dma_start(dbg_q, q_scr),
                   reads=['q_s0', 'q_s1', 'q_s2', 'q_s3', 'q_s4', 'q_s5'], writes=["dbg"], group="dbg")
            sc.dma("sp", lambda e: e.dma_start(dbg_kv, kv_cin.ap()),
                   reads=['kv_s0', 'kv_s1', 'kv_s2', 'kv_s3', 'kv_s4', 'kv_s5'], writes=["dbg"], group="dbg")
            sc.dma("sp", lambda e: e.dma_start(dbg_kvo, kv_cout.ap()),
                   reads=[], writes=["dbg"], waits=[(csem, 1)])
            sc.dma("sp", lambda e: e.dma_start(dbg_wdq, wdq_scr),
                   reads=["wdq_s0", "wdq_s1"], writes=["dbg"], group="dbg")
            sc.dma("sp", lambda e: e.dma_start(dbg_ao, aou[:, :]),
                   reads=[f"slot{h}" for h in range(32)], writes=["dbg"], group="dbg")
            sc.add("sp", lambda e: e.nop(),
                   reads=["out_s0", "out_s1", "dbg"], tag="fin")
        else:
            sc.add("sp", lambda e: e.nop(), reads=["out_s0", "out_s1"],
               tag="fin")
        sc.emit()
        csem_cm.__exit__(None, None, None)
    return nc


def _host_tables():
    btabs = []
    for i in range(4):
        col = np.zeros(24, dtype=np.float32)
        for sl in LO_SLOTS:
            if sl["kind"] == "g" and not (sl["c128"] // 4 < i):
                col[sl["bcol"]] = NEG
        for sl in HI_SLOTS:
            if sl["kind"] == "g" and not (sl["c128"] // 4 < 7 - i):
                col[sl["bcol"]] = NEG
        btabs.append(np.broadcast_to(col, (128, 24)).copy())
    p = np.arange(128)[:, None]
    c = np.arange(512)[None, :]
    m4 = [(r * 128 + p <= c) for r in range(4)]
    masks = np.stack([np.concatenate([m4[2 * pr], m4[2 * pr + 1]], axis=1)
                      for pr in range(2)], axis=0)
    masks = masks.astype(ml_dtypes.bfloat16)
    masks = np.ascontiguousarray(masks.transpose(1, 0, 2))
    return btabs, masks


_NC_CACHE = {}


def make_in_maps(x, wq, wk, wv, wo, wqnt, qsc, qzp, fb, btabs, masks):
    in_maps = []
    for core in range(NCORE):
        b, i = core // 4, core % 4
        lo = x[b, 512 * i:512 * i + 512]
        hi = x[b, 512 * (7 - i):512 * (7 - i) + 512]
        x_my = np.ascontiguousarray(np.concatenate([lo, hi], axis=0))
        in_maps.append(dict(
            x_my=x_my, wq=wq, wk=wk, wv=wv, wo=wo, wqnt=wqnt,
            qsc=qsc, qzp=qzp, fbias=fb, btab_in=btabs[i], masks_in=masks,
        ))
    return in_maps


def kernel(L0_attn_x, L0_attn_wq, L0_attn_wk, L0_attn_wv, L0_attn_wo,
           L0_ffn_weight_q, L0_ffn_scale, L0_ffn_zp, L0_ffn_bias):
    x = np.ascontiguousarray(np.asarray(L0_attn_x, dtype=np.float32))
    wq = np.ascontiguousarray(np.asarray(L0_attn_wq, dtype=np.float32))
    wk = np.ascontiguousarray(np.asarray(L0_attn_wk, dtype=np.float32))
    wv = np.ascontiguousarray(np.asarray(L0_attn_wv, dtype=np.float32))
    wo = np.ascontiguousarray(np.asarray(L0_attn_wo, dtype=np.float32))
    wqnt = np.ascontiguousarray(np.asarray(L0_ffn_weight_q, dtype=np.int32))
    qsc = np.ascontiguousarray(np.asarray(L0_ffn_scale, dtype=np.float32))
    qzp = np.ascontiguousarray(np.asarray(L0_ffn_zp, dtype=np.float32))
    fb = np.ascontiguousarray(
        np.asarray(L0_ffn_bias, dtype=np.float32).reshape(1, D))

    btabs, masks = _host_tables()
    if "nc" not in _NC_CACHE:
        _NC_CACHE["nc"] = build_program()
    nc = _NC_CACHE["nc"]
    in_maps = make_in_maps(x, wq, wk, wv, wo, wqnt, qsc, qzp, fb,
                           btabs, masks)
    res = run_bass_kernel_spmd(nc, in_maps, core_ids=list(range(NCORE)))
    out = np.empty((B, S, D), dtype=np.float32)
    for core in range(NCORE):
        b, i = core // 4, core % 4
        o = res.results[core]["out_my"]
        out[b, 512 * i:512 * i + 512] = o[0:512]
        out[b, 512 * (7 - i):512 * (7 - i) + 512] = o[512:1024]
    return out
